# version 03 (final)
# baseline (speedup 1.0000x reference)
import contextlib
import numpy as np

import concourse.bass as bass
import concourse.mybir as mybir
from concourse.bass_utils import run_bass_kernel_spmd

F32 = mybir.dt.float32
AF = mybir.ActivationFunctionType
ALU = mybir.AluOpType
AX = mybir.AxisListType

B, NTOT, D = 4, 4096, 1024
SEG, NH, DH, PMP = 512, 16, 64, 16
DI = 2816
NCORES = 8
SPC = 4
TPC = SPC * SEG
KCH = D // 128
NBLK = 256
NNB = DI // NBLK
DICH = DI // 128
NEG = -1.0e30
EPS_RMS = float(np.finfo(np.float32).eps)
EPS_LN = 1e-5
THETA = 10000.0
SHUF = list(range(16, 32)) + list(range(0, 16))


def _perm64():
    p = np.zeros(64, dtype=np.int64)
    for b2 in range(2):
        for j in range(16):
            p[32 * b2 + j] = 2 * (16 * b2 + j)
            p[32 * b2 + 16 + j] = 2 * (16 * b2 + j) + 1
    return p


def _tables(pos0):
    inv_freq = THETA ** (-(np.arange(0, DH, 2, dtype=np.float64)) / DH)
    pos = np.arange(pos0, pos0 + TPC, dtype=np.float64)
    pair = np.zeros(64, dtype=np.int64)
    sign = np.zeros(64, dtype=np.float64)
    for b2 in range(2):
        for j in range(16):
            pair[32 * b2 + j] = pair[32 * b2 + 16 + j] = 16 * b2 + j
            sign[32 * b2 + j] = -1.0
            sign[32 * b2 + 16 + j] = 1.0
    ang = pos[None, :] * inv_freq[pair][:, None]
    cos = np.tile(np.cos(ang), (2, 1)).astype(np.float32)
    sin = np.tile(np.sin(ang) * sign[:, None], (2, 1)).astype(np.float32)
    return cos, sin


class P:
    ENGS = ("t", "v", "s", "g", "y")

    def __init__(self, nc, st):
        self.nc = nc
        self.items = []
        self.sem = {e: st.enter_context(nc.semaphore("c_" + e)) for e in self.ENGS}
        self.cnt = {e: 0 for e in self.ENGS}
        self.dsem = {}

    def dma_slot(self, name, st):
        self.dsem[name] = [st.enter_context(self.nc.semaphore("d_" + name)), 0]

    def op(self, eng, fn, inc=False):
        if inc:
            self.cnt[eng] += 1
            sem = self.sem[eng]
            self.items.append((eng, lambda e, fn=fn, sem=sem: fn(e).then_inc(sem, 1)))
        else:
            self.items.append((eng, fn))
        return self.cnt[eng]

    def wait(self, eng, prod, val):
        if val <= 0:
            return
        sem = self.sem[prod]
        self.items.append((eng, lambda e, sem=sem, val=val: e.wait_ge(sem, val)))

    def wait_d(self, eng, slot, val):
        if val <= 0:
            return
        h = self.dsem[slot][0]
        self.items.append((eng, lambda e, h=h, val=val: e.wait_ge(h, val)))

    def drain(self, eng):
        self.items.append((eng, lambda e: e.drain()))

    def dma(self, slot, out_fn, in_fn, guard=None):
        if guard is not None:
            self.wait(*(("y",) + guard))
        rec = self.dsem[slot]
        rec[1] += 16
        h = rec[0]
        self.items.append(
            ("y", lambda e, o=out_fn, i=in_fn, h=h: e.dma_start(o(), i()).then_inc(h, 16)))
        return rec[1]

    def barrier(self):
        vals = dict(self.cnt)
        for e in self.ENGS:
            for pr in self.ENGS:
                if pr != e and vals[pr] > 0:
                    self.wait(e, pr, vals[pr])

    def build(self, block):
        items = self.items
        for eng, deco in (("y", block.sync), ("t", block.tensor), ("v", block.vector),
                          ("s", block.scalar), ("g", block.gpsimd)):
            def body(e, eng=eng):
                for g, fn in items:
                    if g == eng:
                        fn(e)
            deco(body)


def emit_quant(p, xa, xn, acc, tb, sg, out, sml, nbl, eng="v", eng2=None):
    if eng2 is None:
        eng2 = eng
    b3 = lambda f: f().rearrange("p (b k) -> p b k", k=16)
    bc = lambda i: sml(i).unsqueeze(2).broadcast_to([128, nbl, 16])
    p.op(eng, lambda e: e.reduce_max(sml(0), b3(xa), axis=AX.X))
    p.drain(eng)
    p.op(eng, lambda e: e.tensor_scalar_max(sml(1), sml(0), 6e-12))
    p.drain(eng)
    p.op(eng, lambda e: e.reciprocal(sml(2), sml(1)))
    p.drain(eng)
    p.op(eng, lambda e: e.tensor_scalar_mul(sml(2), sml(2), 6.0))
    p.op(eng, lambda e: e.tensor_scalar_mul(sml(1), sml(1), 1.0 / 12.0))
    p.drain(eng)
    p.op(eng, lambda e: e.tensor_tensor(b3(xn), b3(xa), bc(2), op=ALU.mult))
    nxn = p.op(eng, lambda e: e.engine_nop(), inc=True)
    p.drain(eng)
    p.op(eng, lambda e: e.tensor_single_scalar(acc(), xn(), 0.25, op=ALU.is_gt))
    p.drain(eng)
    for t in (0.75, 1.25, 1.75):
        p.op(eng, lambda e, t=t: e.scalar_tensor_tensor(
            acc(), xn(), t, acc(), op0=ALU.is_gt, op1=ALU.add))
        p.drain(eng)
    na = p.op(eng, lambda e: e.engine_nop(), inc=True)
    if eng2 != eng:
        p.wait(eng2, eng, nxn)
    p.op(eng2, lambda e: e.tensor_scalar(tb(), xn(), 5.0, 2.0, op0=ALU.is_gt, op1=ALU.mult))
    p.drain(eng2)
    for t in (2.5, 3.5):
        p.op(eng2, lambda e, t=t: e.scalar_tensor_tensor(
            tb(), xn(), t, tb(), op0=ALU.is_gt, op1=ALU.add))
        p.drain(eng2)
    nb2 = p.op(eng2, lambda e: e.engine_nop(), inc=True)
    if eng2 != eng:
        p.wait(eng, eng2, nb2)
    p.op(eng, lambda e: e.scalar_tensor_tensor(
        xn(), tb(), 2.0, acc(), op0=ALU.mult, op1=ALU.add))
    p.drain(eng)
    p.op(eng, lambda e: e.tensor_tensor(
        acc().rearrange("p (b k) -> p b k", k=16),
        xn().rearrange("p (b k) -> p b k", k=16), bc(1), op=ALU.mult))
    p.drain(eng)
    n = p.op(eng, lambda e: e.tensor_tensor(out(), acc(), sg(), op=ALU.mult), inc=True)
    p.drain(eng)
    return n


DBG_SEG = 0


def preload_set(p, funcs):
    from concourse.hw_specs import get_activation_tables

    def fn(e, funcs=tuple(funcs)):
        for idx, fs in enumerate(get_activation_tables(p.nc.m.arch).values()):
            if all(f in fs for f in funcs):
                return e.add_instruction(mybir.InstLoadActFuncSet(
                    name=p.nc.get_next_instruction_name(), act_func_set_id=idx,
                    ins=[], outs=[]))
        raise ValueError("no table set with %s" % (funcs,))
    p.op("s", fn)


def build_program(debug=False, paranoid=()):
    nc = bass.Bass("TRN2", target_bir_lowering=False)
    st = contextlib.ExitStack()
    E = st.enter_context

    dx = nc.dram_tensor("x", [TPC, D], F32, kind="ExternalInput").ap()
    dwqkv = nc.dram_tensor("wqkv", [24 * 128, KCH * 128], F32, kind="ExternalInput").ap()
    dwout = nc.dram_tensor("wout", [KCH * 128, D], F32, kind="ExternalInput").ap()
    dwg = nc.dram_tensor("wg", [DI, D], F32, kind="ExternalInput").ap()
    dwu = nc.dram_tensor("wu", [DI, D], F32, kind="ExternalInput").ap()
    dwd = nc.dram_tensor("wd", [D, DI], F32, kind="ExternalInput").ap()
    dcos = nc.dram_tensor("cost", [128, TPC], F32, kind="ExternalInput").ap()
    dsin = nc.dram_tensor("sint", [128, TPC], F32, kind="ExternalInput").ap()
    dmask = nc.dram_tensor("mask", [128, 128], F32, kind="ExternalInput").ap()
    dident = nc.dram_tensor("ident", [128, 128], F32, kind="ExternalInput").ap()
    dpmkt = nc.dram_tensor("pmkt", [128, 128], F32, kind="ExternalInput").ap()
    dpmv = nc.dram_tensor("pmv", [PMP, D], F32, kind="ExternalInput").ap()
    dgb = nc.dram_tensor("gb", [128, D], F32, kind="ExternalInput").ap()
    dout = nc.dram_tensor("out", [TPC, D], F32, kind="ExternalOutput").ap()
    dsg = nc.dram_tensor("spill_g", [NNB, 128, KCH * NBLK], F32, kind="ExternalOutput").ap()
    dsu = nc.dram_tensor("spill_u", [NNB, 128, KCH * NBLK], F32, kind="ExternalOutput").ap()
    dsd = nc.dram_tensor("spill_d", [DICH * 128, D], F32, kind="ExternalOutput").ap()

    dbg = {}
    if debug:
        for nm, shp in [("xin", [4 * 128, D]), ("xrms0", [128, D]), ("sml", [128, 64]),
                        ("xfm", [KCH * 128, SEG]), ("qfm", [KCH * 128, SEG]),
                        ("kfm", [KCH * 128, SEG]), ("vtm", [4 * 128, D]),
                        ("avout", [KCH * 128, SEG]), ("x2", [4 * 128, D]),
                        ("xq", [KCH * 128, SEG]), ("hq", [DICH * 128, SEG]),
                        ("wqg", [KCH * 128, DI])]:
            dbg[nm] = nc.dram_tensor("dbg_" + nm, shp, F32, kind="ExternalOutput").ap()

    sb = {}

    def S(name, shape):
        sb[name] = E(nc.sbuf_tensor("sb_" + name, shape, F32))
        return sb[name]

    S("gb", [128, D])
    S("pmkt", [128, 128])
    S("pmv", [PMP, D])
    S("ident", [128, 128])
    S("mask", [128, 128])
    S("cos", [128, TPC])
    S("sin", [128, TPC])
    S("xs", [128, 4 * 1024])
    S("xfm", [128, KCH * SEG])
    S("arenaA", [128, 14848])
    S("arenaB", [128, 8 * 1024])
    S("arenaC", [128, 5 * 1024])
    S("strA", [128, 1024])
    S("strB", [128, 1024])
    S("small", [128, 64])
    S("qsm", [128, 3 * 192])

    A = sb["arenaA"]
    qfm = A[:, 0:4096]
    kfm = A[:, 4096:8192]
    P_A = A[:, 8192:8736]
    P_B = A[:, 8736:9280]
    PT1 = A[:, 9280:11840]
    PT2 = A[:, 11840:14400]
    hq_fm = A[:, 0:DICH * SEG]
    praw = [A[:, 0:2816], A[:, 2816:5632]]
    pqtm = A[:, 5632:8448]
    pstg = A[:, 8448:11264]

    Bv = sb["arenaB"]
    vtm = Bv[:, 0:4096]
    avout = Bv[:, 4096:8192]
    wgu = [[Bv[:, 0:2048], Bv[:, 2048:4096]],
           [Bv[:, 4096:6144], Bv[:, 6144:8192]]]

    C = sb["arenaC"]
    ropeA = C[:, 0:512]
    ropeB = C[:, 512:1024]
    qtb = C[:, 0:1024]
    scr1 = C[:, 1024:2048]
    qxa = C[:, 2048:3072]
    qxn = C[:, 3072:4096]
    qacc = C[:, 4096:5120]

    x2 = None
    S("x2", [128, 4 * 1024])
    x2 = sb["x2"]

    pb = [E(nc.psum_tensor("pb%d" % i, [128, 512], F32)) for i in range(8)]

    p = P(nc, st)
    for slot in ["tab", "xin", "wq0", "wq1", "raw0", "raw1", "spw", "wg0", "wg1",
                 "wu0", "wu1", "out", "dbgd"]:
        p.dma_slot(slot, st)

    def dbg_sync():
        for e in ("t", "v", "s", "g"):
            p.wait_d(e, "dbgd", p.dsem["dbgd"][1])

    tp_bank = (6, 7)
    tpi = [0]
    str_guard = [None, None]
    raw_guard = [None, None]

    def transpose_to(src_ap_fn, dst_ap_fn, wait_v_before=None, rows=128):
        bank = tp_bank[tpi[0] % 2]
        tpi[0] += 1
        if wait_v_before:
            p.wait("t", "v", wait_v_before)
        tt = p.op("t", lambda e, bank=bank, rows=rows: e.transpose(
            pb[bank][0:rows, 0:128], src_ap_fn(), sb["ident"][:]), inc=True)
        p.wait("v", "t", tt)
        ev = p.op("v", lambda e, bank=bank, rows=rows: e.tensor_copy(
            dst_ap_fn(), pb[bank][0:rows, 0:128]), inc=True)
        p.wait("t", "v", ev)
        return ev

    for dst, src in [(lambda: sb["gb"][:], dgb), (lambda: sb["pmkt"][:], dpmkt),
                     (lambda: sb["pmv"][:], dpmv), (lambda: sb["ident"][:], dident),
                     (lambda: sb["mask"][:], dmask), (lambda: sb["cos"][:], dcos),
                     (lambda: sb["sin"][:], dsin)]:
        p.dma("tab", dst, lambda s=src: s)
    tab_t = p.dsem["tab"][1]
    for e in ("t", "v", "s", "g"):
        p.wait_d(e, "tab", tab_t)

    def prolog_matrix(raw_dram, spill_dram, nch, kdim):
        kch = kdim // 128
        last_sign = [0, 0]
        last_spill = [0, 0]
        for c in range(nch):
            sl = c % 2
            wt = p.dma("raw%d" % sl, lambda sl=sl, kdim=kdim: praw[sl][:, 0:kdim],
                       lambda c=c: raw_dram[128 * c:128 * (c + 1), :],
                       guard=("s", last_sign[sl]) if last_sign[sl] else None)
            p.wait_d("s", "raw%d" % sl, wt)
            nslices = (kdim + 1023) // 1024
            for ks in range(nslices):
                kw = min(1024, kdim - 1024 * ks)
                off = 1024 * ks
                p.wait("s", "v", p.cnt["v"])
                p.op("s", lambda e, sl=sl, off=off, kw=kw: e.activation(
                    qxa[:, 0:kw], praw[sl][:, off:off + kw], AF.Abs))
                sgt = p.op("s", lambda e, sl=sl, off=off, kw=kw: e.activation(
                    scr1[:, 0:kw], praw[sl][:, off:off + kw], AF.Sign), inc=True)
                last_sign[sl] = sgt
                p.wait("v", "s", sgt)
                nq = emit_quant(
                    p,
                    xa=lambda kw=kw: qxa[:, 0:kw],
                    xn=lambda kw=kw: qxn[:, 0:kw],
                    acc=lambda kw=kw: qacc[:, 0:kw],
                    tb=lambda kw=kw: qtb[:, 0:kw],
                    sg=lambda kw=kw: scr1[:, 0:kw],
                    out=lambda off=off, kw=kw: pqtm[:, off:off + kw],
                    sml=lambda i, kw=kw: sb["qsm"][:, 192 * i:192 * i + kw // 16],
                    nbl=kw // 16)
            nq_all = p.cnt["v"]
            p.wait_d("v", "spw", p.dsem["spw"][1])
            for t in range(kch):
                transpose_to(
                    lambda t=t: pqtm[:, 128 * t:128 * (t + 1)],
                    lambda t=t: pstg[:, 128 * t:128 * (t + 1)],
                    wait_v_before=nq_all if t == 0 else None)
            ev_all = p.cnt["v"]
            if spill_dram is dsd:
                for t in range(kch):
                    last_spill[0] = p.dma(
                        "spw",
                        lambda c=c, t=t: dsd[128 * t:128 * (t + 1), 128 * c:128 * (c + 1)],
                        lambda t=t: pstg[:, 128 * t:128 * (t + 1)],
                        guard=("v", ev_all) if t == 0 else None)
            else:
                for t in range(kch):
                    last_spill[0] = p.dma(
                        "spw",
                        lambda c=c, t=t, sp=spill_dram: sp[
                            c // 2, :, NBLK * t + 128 * (c % 2):NBLK * t + 128 * (c % 2) + 128],
                        lambda t=t: pstg[:, 128 * t:128 * (t + 1)],
                        guard=("v", ev_all) if t == 0 else None)

    preload_set(p, (AF.Exp, AF.Ln))
    prolog_matrix(dwg, dsg, DICH, D)
    prolog_matrix(dwu, dsu, DICH, D)
    prolog_matrix(dwd, dsd, KCH, DI)
    spill_done = p.dsem["spw"][1]
    for e in ("t", "v", "s", "g"):
        p.wait_d(e, "spw", spill_done)
    p.barrier()

    xbuf = sb["xs"]
    x_t = p.dma("xin", lambda: xbuf[:].rearrange("p (t d) -> p t d", d=D),
                lambda: dx[0:512, :].rearrange("(t p) d -> p t d", p=128))

    def rsqrt4(vs, rs):
        tmp = lambda: sb["small"][:, 48:52]
        p.op("s", lambda e: e.activation(tmp(), vs(), AF.Ln))
        p.drain("s")
        et = p.op("s", lambda e: e.activation(rs(), tmp(), AF.Exp, scale=-0.5), inc=True)
        p.wait("v", "s", et)
        p.op("v", lambda e: e.tensor_mul(sb["small"][:, 52:56], rs(), rs()))
        p.drain("v")
        p.op("v", lambda e: e.tensor_mul(sb["small"][:, 52:56], sb["small"][:, 52:56], vs()))
        p.drain("v")
        p.op("v", lambda e: e.tensor_scalar(sb["small"][:, 52:56], sb["small"][:, 52:56],
                                            -0.5, 1.5, op0=ALU.mult, op1=ALU.add))
        p.drain("v")
        p.op("v", lambda e: e.tensor_mul(rs(), rs(), sb["small"][:, 52:56]))
        p.drain("v")

    def layernorm_stats_and_apply(buf, sub_mean=True, gmul=False, dst=None):
        dst = dst if dst is not None else buf
        for tc in range(4):
            p.op("v", lambda e, tc=tc: e.reduce_sum(
                sb["small"][:, tc:tc + 1], buf[:, 1024 * tc:1024 * (tc + 1)], axis=AX.X))
        p.drain("v")
        sq_done = 0
        for tc in range(4):
            if tc > 0:
                p.wait("s", "v", sq_done)
            sq = p.op("s", lambda e, tc=tc: e.activation(
                scr1[:, 0:1024], buf[:, 1024 * tc:1024 * (tc + 1)], AF.Square), inc=True)
            p.wait("v", "s", sq)
            sq_done = p.op("v", lambda e, tc=tc: e.reduce_sum(
                sb["small"][:, 4 + tc:5 + tc], scr1[:, 0:1024], axis=AX.X), inc=True)
        p.drain("v")
        p.op("v", lambda e: e.tensor_scalar_mul(sb["small"][:, 8:12], sb["small"][:, 0:4], 1.0 / D))
        p.drain("v")
        p.op("v", lambda e: e.tensor_mul(sb["small"][:, 16:20], sb["small"][:, 8:12],
                                         sb["small"][:, 8:12]))
        p.drain("v")
        p.op("v", lambda e: e.scalar_tensor_tensor(
            sb["small"][:, 20:24], sb["small"][:, 4:8], 1.0 / D, sb["small"][:, 16:20],
            op0=ALU.mult, op1=ALU.subtract))
        p.drain("v")
        vv = p.op("v", lambda e: e.tensor_scalar_add(
            sb["small"][:, 20:24], sb["small"][:, 20:24], EPS_LN), inc=True)
        p.wait("s", "v", vv)
        rsqrt4(lambda: sb["small"][:, 20:24], lambda: sb["small"][:, 24:28])
        for tc in range(4):
            p.op("v", lambda e, tc=tc: e.tensor_scalar(
                dst[:, 1024 * tc:1024 * (tc + 1)], buf[:, 1024 * tc:1024 * (tc + 1)],
                sb["small"][:, 8 + tc:9 + tc], sb["small"][:, 24 + tc:25 + tc],
                op0=ALU.subtract, op1=ALU.mult))
        n = p.op("v", lambda e: e.engine_nop(), inc=True)
        p.drain("v")
        return n

    last_out_dma = 0
    for s in range(SPC):
        p.wait_d("v", "xin", x_t)
        p.wait_d("s", "xin", x_t)
        if debug and s == DBG_SEG:
            p.wait_d("y", "xin", x_t)
            p.dma("dbgd", lambda: dbg["xin"].rearrange("(c p) n -> p c n", p=128),
                  lambda: xbuf[:].rearrange("p (c n) -> p c n", n=D))
            dbg_sync()
        sq_done = 0
        for tc in range(4):
            if tc > 0:
                p.wait("s", "v", sq_done)
            sq = p.op("s", lambda e, tc=tc: e.activation(
                scr1[:, 0:1024], xbuf[:, 1024 * tc:1024 * (tc + 1)], AF.Square), inc=True)
            p.wait("v", "s", sq)
            sq_done = p.op("v", lambda e, tc=tc: e.reduce_sum(
                sb["small"][:, tc:tc + 1], scr1[:, 0:1024], axis=AX.X), inc=True)
        p.drain("v")
        vv = p.op("v", lambda e: e.tensor_scalar(
            sb["small"][:, 20:24], sb["small"][:, 0:4], 1.0 / D, EPS_RMS,
            op0=ALU.mult, op1=ALU.add), inc=True)
        p.wait("s", "v", vv)
        rsqrt4(lambda: sb["small"][:, 20:24], lambda: sb["small"][:, 24:28])
        for tc in range(4):
            nrm = p.op("v", lambda e, tc=tc: e.scalar_tensor_tensor(
                scr1[:, 0:1024], xbuf[:, 1024 * tc:1024 * (tc + 1)],
                sb["small"][:, 24 + tc:25 + tc], sb["gb"][:],
                op0=ALU.mult, op1=ALU.mult), inc=True)
            p.drain("v")
            if debug and s == DBG_SEG and tc == 0:
                p.wait("y", "v", nrm)
                p.dma("dbgd", lambda: dbg["xrms0"], lambda: scr1[:, 0:1024])
                p.dma("dbgd", lambda: dbg["sml"], lambda: sb["small"][:])
                dbg_sync()
            for k in range(KCH):
                transpose_to(
                    lambda k=k: scr1[:, 128 * k:128 * (k + 1)],
                    lambda k=k, tc=tc: sb["xfm"][:, 512 * k + 128 * tc:512 * k + 128 * (tc + 1)],
                    wait_v_before=nrm if k == 0 else None)
        xfm_v = p.cnt["v"]
        if "rms" in paranoid:
            p.barrier()

        tnow = p.cnt["t"]
        strg = [("t", tnow), ("t", tnow)]
        for c in range(24):
            sl = c % 2
            buf = sb["strA"] if sl == 0 else sb["strB"]
            wt = p.dma("wq%d" % sl, lambda buf=buf: buf[:],
                       lambda c=c: dwqkv[128 * c:128 * (c + 1), :],
                       guard=strg[sl])
            p.wait_d("t", "wq%d" % sl, wt)
            if c == 0:
                p.wait("t", "v", xfm_v)
            bank = c % 3
            for k in range(KCH):
                mmt = p.op("t", lambda e, buf=buf, k=k, bank=bank: e.matmul(
                    pb[bank][:, 0:512], buf[:, 128 * k:128 * (k + 1)],
                    sb["xfm"][:, 512 * k:512 * (k + 1)],
                    start=(k == 0), stop=(k == KCH - 1)), inc=(k == KCH - 1))
            strg[sl] = ("t", mmt)
            p.wait("v", "t", mmt)
            p.drain("v")
            if c < 16:
                dst = qfm if c < 8 else kfm
                cc = c % 8
                p.op("v", lambda e, bank=bank, s=s: e.tensor_tensor(
                    ropeA[:], pb[bank][:, 0:512],
                    sb["cos"][:, SEG * s:SEG * (s + 1)], op=ALU.mult))
                shf = p.op("v", lambda e, bank=bank: e.stream_shuffle(
                    ropeB[:], pb[bank][:, 0:512], SHUF), inc=True)
                p.wait("g", "v", shf)
                g1 = p.op("g", lambda e, s=s: e.tensor_tensor(
                    ropeB[:], ropeB[:], sb["sin"][:, SEG * s:SEG * (s + 1)],
                    op=ALU.mult), inc=True)
                p.wait("v", "g", g1)
                p.drain("v")
                ev = p.op("v", lambda e, dst=dst, cc=cc: e.tensor_tensor(
                    dst[:, 512 * cc:512 * (cc + 1)], ropeA[:], ropeB[:], op=ALU.add),
                    inc=True)
                p.wait("t", "v", ev)
            else:
                cc = c - 16
                ev = p.op("v", lambda e, bank=bank: e.tensor_copy(
                    ropeA[:], pb[bank][:, 0:512]), inc=True)
                p.wait("t", "v", ev)
                p.drain("v")
                for t4 in range(4):
                    transpose_to(
                        lambda t4=t4: ropeA[:, 128 * t4:128 * (t4 + 1)],
                        lambda t4=t4, cc=cc: vtm[:, 1024 * t4 + 128 * cc:1024 * t4 + 128 * (cc + 1)])
        if debug and s == DBG_SEG:
            vr = p.cnt["v"]
            p.dma("dbgd", lambda: dbg["xfm"].rearrange("(c p) n -> p c n", p=128),
                  lambda: sb["xfm"][:].rearrange("p (c n) -> p c n", n=SEG),
                  guard=("v", vr))
            p.dma("dbgd", lambda: dbg["qfm"].rearrange("(c p) n -> p c n", p=128),
                  lambda: qfm.rearrange("p (c n) -> p c n", n=SEG))
            p.dma("dbgd", lambda: dbg["kfm"].rearrange("(c p) n -> p c n", p=128),
                  lambda: kfm.rearrange("p (c n) -> p c n", n=SEG))
            p.dma("dbgd", lambda: dbg["vtm"].rearrange("(c p) n -> p c n", p=128),
                  lambda: vtm.rearrange("p (c n) -> p c n", n=D))
            dbg_sync()

        if "qkv" in paranoid:
            p.barrier()
        halves = (slice(0, 64), slice(64, 128))
        for pr in range(8):
            Ps = (P_A, P_B)
            PTs = (PT1, PT2)
            last_exp = p.cnt["s"]
            last_ptt = p.cnt["t"]
            for qc in range(4):
                ext = 128 * (qc + 1)
                p.wait("t", "s", last_exp)
                for sps in range(2):
                    p.op("t", lambda e, sps=sps, qc=qc, ext=ext, pr=pr: e.matmul(
                        pb[sps][:, 0:ext],
                        qfm[halves[sps], 512 * pr + 128 * qc:512 * pr + 128 * (qc + 1)],
                        kfm[halves[sps], 512 * pr:512 * pr + ext],
                        start=True, stop=True))
                    p.op("t", lambda e, sps=sps, qc=qc, pr=pr: e.matmul(
                        pb[2][:, 16 * sps:16 * (sps + 1)],
                        qfm[halves[sps], 512 * pr + 128 * qc:512 * pr + 128 * (qc + 1)],
                        sb["pmkt"][halves[sps], 16 * pr:16 * (pr + 1)],
                        start=True, stop=True), inc=(sps == 1))
                smm = p.cnt["t"]
                p.wait("v", "t", smm)
                for sps in range(2):
                    p.op("v", lambda e, sps=sps, qc=qc: e.tensor_tensor(
                        pb[sps][:, 128 * qc:128 * (qc + 1)],
                        pb[sps][:, 128 * qc:128 * (qc + 1)], sb["mask"][:], op=ALU.add))
                p.drain("v")
                for sps in range(2):
                    p.op("v", lambda e, sps=sps, ext=ext: e.reduce_max(
                        sb["small"][:, 32 + sps:33 + sps], pb[sps][:, 0:ext],
                        axis=AX.X, negate=True))
                    p.op("v", lambda e, sps=sps: e.reduce_max(
                        sb["small"][:, 34 + sps:35 + sps], pb[2][:, 16 * sps:16 * (sps + 1)],
                        axis=AX.X, negate=True))
                p.drain("v")
                p.op("v", lambda e: e.tensor_tensor(
                    sb["small"][:, 32:34], sb["small"][:, 32:34], sb["small"][:, 34:36],
                    op=ALU.min))
                p.drain("v")
                nmv = p.op("v", lambda e: e.tensor_scalar_mul(
                    sb["small"][:, 36:38], sb["small"][:, 32:34], 0.125), inc=True)
                p.wait("s", "v", nmv)
                p.wait("s", "t", last_ptt)
                for sps in range(2):
                    p.op("s", lambda e, sps=sps, ext=ext: e.activation(
                        Ps[sps][:, 0:ext], pb[sps][:, 0:ext], AF.Exp, scale=0.125,
                        bias=sb["small"][:, 36 + sps:37 + sps],
                        accum_out=sb["small"][:, 40 + 2 * sps:41 + 2 * sps]))
                    p.op("s", lambda e, sps=sps: e.activation(
                        Ps[sps][:, 512:528], pb[2][:, 16 * sps:16 * (sps + 1)], AF.Exp,
                        scale=0.125, bias=sb["small"][:, 36 + sps:37 + sps],
                        accum_out=sb["small"][:, 41 + 2 * sps:42 + 2 * sps]),
                        inc=(sps == 1))
                last_exp = p.cnt["s"]
                p.wait("v", "s", last_exp)
                p.op("v", lambda e: e.tensor_tensor(
                    sb["small"][:, 44:45], sb["small"][:, 40:41], sb["small"][:, 41:42],
                    op=ALU.add))
                p.op("v", lambda e: e.tensor_tensor(
                    sb["small"][:, 45:46], sb["small"][:, 42:43], sb["small"][:, 43:44],
                    op=ALU.add))
                p.drain("v")
                p.op("v", lambda e: e.reciprocal(sb["small"][:, 46:48], sb["small"][:, 44:46]))
                p.drain("v")
                for sps in range(2):
                    p.op("v", lambda e, sps=sps, ext=ext: e.tensor_scalar_mul(
                        Ps[sps][:, 0:ext], Ps[sps][:, 0:ext],
                        sb["small"][:, 46 + sps:47 + sps]))
                    p.op("v", lambda e, sps=sps: e.tensor_scalar_mul(
                        Ps[sps][:, 512:528], Ps[sps][:, 512:528],
                        sb["small"][:, 46 + sps:47 + sps]), inc=(sps == 1))
                nrm = p.cnt["v"]
                p.drain("v")
                for hi in range(2):
                    transpose_to(lambda hi=hi: Ps[hi][:, 512:528],
                                 lambda hi=hi, qc=qc: PTs[hi][0:16, 128 * qc:128 * (qc + 1)],
                                 wait_v_before=nrm if hi == 0 else None, rows=16)
                    for kb in range(qc + 1):
                        transpose_to(
                            lambda hi=hi, kb=kb: Ps[hi][:, 128 * kb:128 * (kb + 1)],
                            lambda hi=hi, kb=kb, qc=qc: PTs[hi][:, 512 * (kb + 1) + 128 * qc:512 * (kb + 1) + 128 * (qc + 1)])
                last_ptt = p.cnt["t"]
                if "qc" in paranoid:
                    p.barrier()
            pt_v = p.cnt["v"]
            p.wait("t", "v", pt_v)
            for hi in range(2):
                h = 2 * pr + hi
                orow = slice(64 * hi, 64 * (hi + 1))
                p.op("t", lambda e, h=h, orow=orow, hi=hi: e.matmul(
                    pb[3][orow, 0:512], sb["pmv"][:, 64 * h:64 * (h + 1)],
                    PTs[hi][0:16, 0:512], start=True, stop=False))
                for kb in range(4):
                    p.op("t", lambda e, h=h, kb=kb, orow=orow, hi=hi: e.matmul(
                        pb[3][orow, 128 * kb:512],
                        vtm[:, 1024 * kb + 64 * h:1024 * kb + 64 * (h + 1)],
                        PTs[hi][:, 512 * (kb + 1) + 128 * kb:512 * (kb + 2)],
                        start=False, stop=(kb == 3)),
                        inc=(hi == 1 and kb == 3))
            avt = p.cnt["t"]
            p.wait("v", "t", avt)
            ev = p.op("v", lambda e, pr=pr: e.tensor_copy(
                avout[:, 512 * pr:512 * (pr + 1)], pb[3][:, 0:512]), inc=True)
            p.wait("t", "v", ev)
            if "pair" in paranoid:
                p.barrier()
        if debug and s == DBG_SEG:
            p.dma("dbgd", lambda: dbg["avout"].rearrange("(c p) n -> p c n", p=128),
                  lambda: avout.rearrange("p (c n) -> p c n", n=SEG),
                  guard=("v", p.cnt["v"]))
            dbg_sync()

        p.wait("t", "s", p.cnt["s"])
        p.wait("t", "v", p.cnt["v"])
        tnow = p.cnt["t"]
        strg = [("t", tnow), ("t", tnow)]
        for k in range(KCH):
            sl = k % 2
            buf = sb["strA"] if sl == 0 else sb["strB"]
            wt = p.dma("wq%d" % sl, lambda buf=buf: buf[:],
                       lambda k=k: dwout[128 * k:128 * (k + 1), :], guard=strg[sl])
            p.wait_d("t", "wq%d" % sl, wt)
            for tc in range(4):
                for half in range(2):
                    mmt = p.op("t", lambda e, k=k, tc=tc, half=half, buf=buf: e.matmul(
                        pb[2 * tc + half][:, 0:512],
                        avout[:, 512 * k + 128 * tc:512 * k + 128 * (tc + 1)],
                        buf[:, 512 * half:512 * (half + 1)],
                        start=(k == 0), stop=(k == KCH - 1)),
                        inc=(tc == 3 and half == 1))
            strg[sl] = ("t", mmt)
        wot = p.cnt["t"]
        p.wait("v", "t", wot)
        for tc in range(4):
            for half in range(2):
                p.op("v", lambda e, tc=tc, half=half: e.tensor_tensor(
                    x2[:, 1024 * tc + 512 * half:1024 * tc + 512 * (half + 1)],
                    xbuf[:, 1024 * tc + 512 * half:1024 * tc + 512 * (half + 1)],
                    pb[2 * tc + half][:, 0:512], op=ALU.add),
                    inc=(tc == 3 and half == 1))
        x1v = p.cnt["v"]
        p.drain("v")
        p.wait("s", "v", x1v)
        x2v = layernorm_stats_and_apply(x2)
        if debug and s == DBG_SEG:
            p.dma("dbgd", lambda: dbg["x2"].rearrange("(c p) n -> p c n", p=128),
                  lambda: x2[:].rearrange("p (c n) -> p c n", n=D), guard=("v", x2v))
            dbg_sync()

        if "wout" in paranoid:
            p.barrier()
        for tc in range(4):
            xsl = lambda tc=tc: x2[:, 1024 * tc:1024 * (tc + 1)]
            p.wait("s", "v", p.cnt["v"])
            p.op("s", lambda e, xsl=xsl: e.activation(qxa[:, 0:1024], xsl(), AF.Abs))
            sg = p.op("s", lambda e, xsl=xsl: e.activation(scr1[:, 0:1024], xsl(), AF.Sign),
                      inc=True)
            p.wait("v", "s", sg)
            nq = emit_quant(
                p, xa=lambda: qxa[:, 0:1024], xn=lambda: qxn[:, 0:1024],
                acc=lambda: qacc[:, 0:1024], tb=lambda: qtb[:, 0:1024],
                sg=lambda: scr1[:, 0:1024], out=lambda: qxn[:, 0:1024],
                sml=lambda i: sb["qsm"][:, 192 * i:192 * i + 64], nbl=64)
            for k in range(KCH):
                transpose_to(
                    lambda k=k: qxn[:, 128 * k:128 * (k + 1)],
                    lambda k=k, tc=tc: sb["xfm"][:, 512 * k + 128 * tc:512 * k + 128 * (tc + 1)],
                    wait_v_before=nq if k == 0 else None)
        xq_v = p.cnt["v"]
        if debug and s == DBG_SEG:
            p.dma("dbgd", lambda: dbg["xq"].rearrange("(c p) n -> p c n", p=128),
                  lambda: sb["xfm"][:].rearrange("p (c n) -> p c n", n=SEG),
                  guard=("v", xq_v))
            dbg_sync()

        if "xq" in paranoid:
            p.barrier()
        tnow = p.cnt["t"]
        gug = [[("t", tnow), ("t", tnow)], [("t", tnow), ("t", tnow)]]
        silu_t = p.cnt["s"]
        hv_t = p.cnt["v"]
        for nb in range(NNB):
            sl = nb % 2
            wts = []
            for gi, spill in enumerate((dsg, dsu)):
                wts.append(p.dma(
                    ("wg%d" % sl) if gi == 0 else ("wu%d" % sl),
                    lambda gi=gi, sl=sl: wgu[gi][sl][:],
                    lambda spill=spill, nb=nb: spill[nb],
                    guard=gug[gi][sl]))
            p.wait_d("t", "wg%d" % sl, wts[0])
            p.wait_d("t", "wu%d" % sl, wts[1])
            if nb == 0:
                p.wait("t", "v", xq_v)
            for tc in range(4):
                p.wait("t", "s", silu_t)
                p.wait("t", "v", hv_t)
                for k in range(KCH):
                    p.op("t", lambda e, sl=sl, tc=tc, k=k: e.matmul(
                        pb[0][:, 0:NBLK],
                        sb["xfm"][:, 512 * k + 128 * tc:512 * k + 128 * (tc + 1)],
                        wgu[0][sl][:, NBLK * k:NBLK * (k + 1)],
                        start=(k == 0), stop=(k == KCH - 1)), inc=(k == KCH - 1))
                for k in range(KCH):
                    p.op("t", lambda e, sl=sl, tc=tc, k=k: e.matmul(
                        pb[1][:, 0:NBLK],
                        sb["xfm"][:, 512 * k + 128 * tc:512 * k + 128 * (tc + 1)],
                        wgu[1][sl][:, NBLK * k:NBLK * (k + 1)],
                        start=(k == 0), stop=(k == KCH - 1)), inc=(k == KCH - 1))
                gmt = p.cnt["t"]
                p.wait("s", "t", gmt)
                p.wait("s", "v", p.cnt["v"])
                silu_t = p.op("s", lambda e: e.activation(
                    scr1[:, 0:NBLK], pb[0][:, 0:NBLK], AF.Exp, scale=-1.0), inc=True)
                p.wait("v", "s", silu_t)
                p.op("v", lambda e: e.tensor_scalar_add(
                    scr1[:, 0:NBLK], scr1[:, 0:NBLK], 1.0))
                p.drain("v")
                p.op("v", lambda e: e.reciprocal(scr1[:, 0:NBLK], scr1[:, 0:NBLK]))
                p.drain("v")
                p.op("v", lambda e: e.tensor_tensor(
                    qxn[:, 0:NBLK], scr1[:, 0:NBLK], pb[0][:, 0:NBLK], op=ALU.mult))
                p.drain("v")
                hv_t = p.op("v", lambda e: e.tensor_tensor(
                    qxn[:, 0:NBLK], qxn[:, 0:NBLK], pb[1][:, 0:NBLK], op=ALU.mult),
                    inc=True)
                p.drain("v")
                p.wait("s", "v", hv_t)
                p.op("s", lambda e: e.activation(qxa[:, 0:NBLK], qxn[:, 0:NBLK], AF.Abs))
                sg = p.op("s", lambda e: e.activation(
                    scr1[:, 0:NBLK], qxn[:, 0:NBLK], AF.Sign), inc=True)
                p.wait("v", "s", sg)
                nq = emit_quant(
                    p, xa=lambda: qxa[:, 0:NBLK], xn=lambda: qxa[:, 256:256 + NBLK],
                    acc=lambda: qacc[:, 0:NBLK], tb=lambda: qtb[:, 0:NBLK],
                    sg=lambda: scr1[:, 0:NBLK], out=lambda: qacc[:, 0:NBLK],
                    sml=lambda i: sb["qsm"][:, 192 * i:192 * i + NBLK // 16],
                    nbl=NBLK // 16)
                for j in range(NBLK // 128):
                    kc = (NBLK * nb) // 128 + j
                    transpose_to(
                        lambda j=j: qacc[:, 128 * j:128 * (j + 1)],
                        lambda kc=kc, tc=tc: hq_fm[:, 512 * kc + 128 * tc:512 * kc + 128 * (tc + 1)],
                        wait_v_before=nq if j == 0 else None)
            gug[0][sl] = ("t", p.cnt["t"])
            gug[1][sl] = ("t", p.cnt["t"])
            if "nb" in paranoid:
                p.barrier()
        hq_v = p.cnt["v"]
        if debug and s == DBG_SEG:
            p.dma("dbgd", lambda: dbg["hq"].rearrange("(c p) n -> p c n", p=128),
                  lambda: hq_fm.rearrange("p (c n) -> p c n", n=SEG),
                  guard=("v", hq_v))
            dbg_sync()

        p.wait("t", "v", hq_v)
        p.wait("t", "s", silu_t)
        tnow = p.cnt["t"]
        strg = [("t", tnow), ("t", tnow)]
        for kc in range(DICH):
            sl = kc % 2
            buf = sb["strA"] if sl == 0 else sb["strB"]
            wt = p.dma("wq%d" % sl, lambda buf=buf: buf[:],
                       lambda kc=kc: dsd[128 * kc:128 * (kc + 1), :], guard=strg[sl])
            p.wait_d("t", "wq%d" % sl, wt)
            for tc in range(4):
                for half in range(2):
                    mmt = p.op("t", lambda e, kc=kc, tc=tc, half=half, buf=buf: e.matmul(
                        pb[2 * tc + half][:, 0:512],
                        hq_fm[:, 512 * kc + 128 * tc:512 * kc + 128 * (tc + 1)],
                        buf[:, 512 * half:512 * (half + 1)],
                        start=(kc == 0), stop=(kc == DICH - 1)),
                        inc=(tc == 3 and half == 1))
            strg[sl] = ("t", mmt)
        dwt = p.cnt["t"]
        if "down" in paranoid:
            p.barrier()
        p.wait("v", "t", dwt)
        for tc in range(4):
            for half in range(2):
                p.op("v", lambda e, tc=tc, half=half: e.tensor_tensor(
                    xbuf[:, 1024 * tc + 512 * half:1024 * tc + 512 * (half + 1)],
                    x2[:, 1024 * tc + 512 * half:1024 * tc + 512 * (half + 1)],
                    pb[2 * tc + half][:, 0:512], op=ALU.add),
                    inc=(tc == 3 and half == 1))
        x3v = p.cnt["v"]
        p.drain("v")
        p.wait("s", "v", x3v)
        outv = layernorm_stats_and_apply(xbuf)
        last_out_dma = p.dma(
            "out",
            lambda s=s: dout[512 * s:512 * (s + 1), :].rearrange("(t p) d -> p t d", p=128),
            lambda: xbuf[:].rearrange("p (t d) -> p t d", d=D), guard=("v", outv))
        if s + 1 < SPC:
            p.wait_d("y", "out", last_out_dma)
            x_t = p.dma("xin", lambda: xbuf[:].rearrange("p (t d) -> p t d", d=D),
                        lambda s=s: dx[512 * (s + 1):512 * (s + 2), :].rearrange(
                            "(t p) d -> p t d", p=128))

    p.wait_d("y", "out", last_out_dma)
    if debug:
        p.wait_d("y", "dbgd", p.dsem["dbgd"][1])

    with nc.Block() as block:
        p.build(block)
    st.close()
    return nc



def _host_inputs(L0_attn_seq, L0_attn_g_norm, L0_attn_W_qkv, L0_attn_W_out,
                 L0_attn_pm, L0_ffn_w_gate, L0_ffn_w_up, L0_ffn_w_down):
    pm64 = _perm64()
    qk_idx = np.concatenate([h * 64 + pm64 for h in range(NH)])
    W = np.asarray(L0_attn_W_qkv, dtype=np.float32)
    Wcat = np.concatenate(
        [W[:, 0:1024][:, qk_idx], W[:, 1024:2048][:, qk_idx], W[:, 2048:3072]], axis=1)
    wqkv_dev = np.ascontiguousarray(
        Wcat.reshape(KCH, 128, 24, 128).transpose(2, 1, 0, 3).reshape(24 * 128, KCH * 128))
    wout_dev = np.ascontiguousarray(np.asarray(L0_attn_W_out, np.float32))
    pm = np.asarray(L0_attn_pm, np.float32)
    pmkt = np.zeros((128, 128), np.float32)
    for pr in range(8):
        for hi in range(2):
            h = 2 * pr + hi
            pmkt[64 * hi:64 * (hi + 1), 16 * pr:16 * (pr + 1)] = pm[0][h][:, pm64].T
    pmv = np.ascontiguousarray(pm[1].transpose(1, 0, 2).reshape(PMP, NH * DH))
    gb = np.ascontiguousarray(
        np.broadcast_to(np.asarray(L0_attn_g_norm, np.float32), (128, D)))
    mask = np.zeros((128, 128), np.float32)
    r = np.arange(128)
    mask[r[:, None] < r[None, :]] = NEG
    ident = np.eye(128, dtype=np.float32)
    wg = np.ascontiguousarray(np.asarray(L0_ffn_w_gate, np.float32))
    wu = np.ascontiguousarray(np.asarray(L0_ffn_w_up, np.float32))
    wd = np.ascontiguousarray(np.asarray(L0_ffn_w_down, np.float32))
    seq = np.asarray(L0_attn_seq, np.float32)

    in_maps = []
    for c in range(NCORES):
        b = c // 2
        pos0 = (c % 2) * TPC
        cos, sin = _tables(pos0)
        in_maps.append({
            "x": np.ascontiguousarray(seq[b, pos0:pos0 + TPC, :]),
            "wqkv": wqkv_dev, "wout": wout_dev, "wg": wg, "wu": wu, "wd": wd,
            "cost": cos, "sint": sin, "mask": mask, "ident": ident,
            "pmkt": pmkt, "pmv": pmv, "gb": gb,
        })
    return in_maps


_NC_CACHE = {}


def kernel(L0_attn_seq, L0_attn_g_norm, L0_attn_W_qkv, L0_attn_W_out,
           L0_attn_pm, L0_ffn_w_gate, L0_ffn_w_up, L0_ffn_w_down):
    in_maps = _host_inputs(L0_attn_seq, L0_attn_g_norm, L0_attn_W_qkv,
                           L0_attn_W_out, L0_attn_pm, L0_ffn_w_gate,
                           L0_ffn_w_up, L0_ffn_w_down)
    if "nc" not in _NC_CACHE:
        _NC_CACHE["nc"] = build_program()
    nc = _NC_CACHE["nc"]
    res = run_bass_kernel_spmd(nc, in_maps, core_ids=list(range(NCORES)))
    out = np.zeros((B, NTOT, D), dtype=np.float32)
    for c in range(NCORES):
        b = c // 2
        pos0 = (c % 2) * TPC
        out[b, pos0:pos0 + TPC, :] = np.asarray(res.results[c]["out"])
    return out
